# version 03
# baseline (speedup 1.0000x reference)
import numpy as np
import ml_dtypes
import concourse.bass as bass
import concourse.mybir as mybir
from concourse.bass_utils import run_bass_kernel_spmd

F32 = mybir.dt.float32
BF16 = mybir.dt.bfloat16
I32 = mybir.dt.int32
AF = mybir.ActivationFunctionType
ALU = mybir.AluOpType
AX = mybir.AxisListType

D = 2048
H = 16
B_CORE = 2
S = 2048
TT = 512
NTILE = (B_CORE * S) // TT
NMLP = 16384
NKC = D // 128
INV_SQRT_HD = float(1.0 / np.sqrt(128.0))

MAGIC = 0x5F3759DF
SIGN32 = -0x80000000


def _perm_half(n):
    p = np.empty(n, dtype=np.int64)
    half = n // 2
    p[:half] = 2 * np.arange(half)
    p[half:] = 2 * np.arange(half) + 1
    return p


PERM_D = _perm_half(D)
PERM_N = _perm_half(NMLP)


def _consts():
    d = np.arange(128)
    ang = 2.0 * np.pi * np.outer(d, d) / 128.0
    ccos = np.cos(ang).astype(ml_dtypes.bfloat16)
    csinn = (-np.sin(ang)).astype(ml_dtypes.bfloat16)
    cicos = (np.cos(ang) / 128.0).astype(ml_dtypes.bfloat16)
    cisinn = (-np.sin(ang) / 128.0).astype(ml_dtypes.bfloat16)
    ident = np.eye(128, dtype=ml_dtypes.bfloat16)
    ones128 = np.ones((128, 1), dtype=ml_dtypes.bfloat16)
    ones1x128 = np.ones((1, 128), dtype=np.float32)
    return ccos, csinn, cicos, cisinn, ident, ones128, ones1x128


class Builder:
    ENGINES = ("sync", "tensor", "vector", "scalar", "gpsimd")

    def __init__(self, nc):
        self.nc = nc
        self.ops = {e: [] for e in self.ENGINES}
        self.cnt = {}
        self.sems = {}
        self.guard = {}

    def sem(self, name):
        self.sems[name] = self.nc.alloc_semaphore(name)
        self.cnt[name] = 0
        return name

    def E(self, eng, fn, *incs):
        for s, n in incs:
            self.cnt[s] += n
        self.ops[eng].append(("op", fn, incs))

    def W(self, eng, sem, val=None):
        v = self.cnt[sem] if val is None else val
        self.ops[eng].append(("wait", sem, v))

    def need(self, eng, key):
        for s, v in self.guard.pop(key, []):
            self.W(eng, s, v)

    def reads(self, key, sem):
        self.guard.setdefault(key, []).append((sem, self.cnt[sem]))

    def replay(self, block):
        def mk(eng_name):
            def run(eng):
                for entry in self.ops[eng_name]:
                    if entry[0] == "wait":
                        _, s, v = entry
                        if v > 0:
                            eng.wait_ge(self.sems[s], v)
                    else:
                        _, fn, incs = entry
                        inst = fn(eng)
                        for s, n in incs:
                            inst.then_inc(self.sems[s], n)
            return run
        for e in self.ENGINES:
            getattr(block, e)(mk(e))


def build_program():
    nc = bass.Bass("TRN2", target_bir_lowering=False)

    x_in = nc.dram_tensor("x", [B_CORE, D, S], F32, kind="ExternalInput").ap()
    wq_in = nc.dram_tensor("wqT", [D, D], F32, kind="ExternalInput").ap()
    wkv_in = nc.dram_tensor("wkvT", [D, D], F32, kind="ExternalInput").ap()
    wo_in = nc.dram_tensor("woT", [D, D], F32, kind="ExternalInput").ap()
    gpak_in = nc.dram_tensor("gpak", [1024, NMLP], I32, kind="ExternalInput").ap()
    gscl_in = nc.dram_tensor("gscl", [64, NMLP], F32, kind="ExternalInput").ap()
    upak_in = nc.dram_tensor("upak", [1024, NMLP], I32, kind="ExternalInput").ap()
    uscl_in = nc.dram_tensor("uscl", [64, NMLP], F32, kind="ExternalInput").ap()
    dpak_in = nc.dram_tensor("dpak", [8192, D], I32, kind="ExternalInput").ap()
    dscl_in = nc.dram_tensor("dscl", [512, D], F32, kind="ExternalInput").ap()
    ccos_in = nc.dram_tensor("ccos", [128, 128], BF16, kind="ExternalInput").ap()
    csinn_in = nc.dram_tensor("csinn", [128, 128], BF16, kind="ExternalInput").ap()
    cicos_in = nc.dram_tensor("cicos", [128, 128], BF16, kind="ExternalInput").ap()
    cisinn_in = nc.dram_tensor("cisinn", [128, 128], BF16, kind="ExternalInput").ap()
    ident_in = nc.dram_tensor("ident", [128, 128], BF16, kind="ExternalInput").ap()
    ones128_in = nc.dram_tensor("ones128", [128, 1], BF16, kind="ExternalInput").ap()
    ones1x128_in = nc.dram_tensor("ones1x128", [1, 128], F32, kind="ExternalInput").ap()

    y_out = nc.dram_tensor("y", [B_CORE, D, S], F32, kind="ExternalOutput").ap()

    wq_bf = nc.dram_tensor("wq_bf", [4, 128, 16, 512], BF16, kind="Internal").ap()
    wkv_bf = nc.dram_tensor("wkv_bf", [4, 128, 16, 512], BF16, kind="Internal").ap()
    wo_bf = nc.dram_tensor("wo_bf", [16, 128, 16, 128], BF16, kind="Internal").ap()
    wg_bf = nc.dram_tensor("wg_bf", [128, 128, 16, 128], BF16, kind="Internal").ap()
    wu_bf = nc.dram_tensor("wu_bf", [128, 128, 16, 128], BF16, kind="Internal").ap()
    wd_bf = nc.dram_tensor("wd_bf", [16, 4, 128, 32, 128], BF16, kind="Internal").ap()

    x_sb = nc.alloc_sbuf_tensor("x_sb", [128, 16, TT], F32)
    b1 = nc.alloc_sbuf_tensor("b1", [128, 16, TT], BF16)
    qT = nc.alloc_sbuf_tensor("qT", [128, 16, TT], BF16)
    kT = nc.alloc_sbuf_tensor("kT", [128, 17, TT], BF16)
    wsA = nc.alloc_sbuf_tensor("wsA", [128, 8192], BF16)
    wsB = nc.alloc_sbuf_tensor("wsB", [128, 8192], BF16)
    wsC = nc.alloc_sbuf_tensor("wsC", [128, 8192], BF16)
    inter = nc.alloc_sbuf_tensor("inter", [128, 32, TT], BF16)

    sq_q = nc.alloc_sbuf_tensor("sq_q", [128, TT], BF16)
    sq_k = nc.alloc_sbuf_tensor("sq_k", [128, TT], BF16)
    qtok = nc.alloc_sbuf_tensor("qtok", [128, TT], BF16)
    ktok = nc.alloc_sbuf_tensor("ktok", [128, TT], BF16)
    qfr = nc.alloc_sbuf_tensor("qfr", [128, TT], BF16)
    qfi = nc.alloc_sbuf_tensor("qfi", [128, TT], BF16)
    kfr = nc.alloc_sbuf_tensor("kfr", [128, TT], BF16)
    kfi = nc.alloc_sbuf_tensor("kfi", [128, TT], BF16)
    qvr = nc.alloc_sbuf_tensor("qvr", [128, TT], BF16)
    qvi = nc.alloc_sbuf_tensor("qvi", [128, TT], BF16)
    qvt = nc.alloc_sbuf_tensor("qvt", [128, TT], BF16)
    gel = nc.alloc_sbuf_tensor("gel", [128, 2, TT], BF16)
    tmp1 = nc.alloc_sbuf_tensor("tmp1", [128, TT], F32)
    xb2 = nc.alloc_sbuf_tensor("xb2", [128, 2, TT], BF16)
    xsq = nc.alloc_sbuf_tensor("xsq", [128, 2, TT], BF16)

    ssq = nc.alloc_sbuf_tensor("ssq", [128, 8], F32)
    ny = nc.alloc_sbuf_tensor("ny", [128, 8], F32)
    na = nc.alloc_sbuf_tensor("na", [128, 8], F32)
    nh = nc.alloc_sbuf_tensor("nh", [128, 8], F32)
    rqk = nc.alloc_sbuf_tensor("rqk", [128, 8], F32)
    carry = nc.alloc_sbuf_tensor("carry", [128, 16], F32)
    cred = nc.alloc_sbuf_tensor("cred", [128, 1], F32)

    m_neg = nc.alloc_sbuf_tensor("m_neg", [1, TT], BF16)
    meanf = nc.alloc_sbuf_tensor("meanf", [1, TT], F32)[:]
    msq = nc.alloc_sbuf_tensor("msq", [1, TT], F32)[:]
    varb = nc.alloc_sbuf_tensor("varb", [1, TT], F32)[:]
    l_ny = nc.alloc_sbuf_tensor("l_ny", [1, TT], F32)[:]
    l_na = nc.alloc_sbuf_tensor("l_na", [1, TT], F32)[:]
    l_nh = nc.alloc_sbuf_tensor("l_nh", [1, TT], F32)[:]
    rstd = nc.alloc_sbuf_tensor("rstd", [1, TT], F32)
    wbar = nc.alloc_sbuf_tensor("wbar", [1, 2 * D], BF16)

    ccos = nc.alloc_sbuf_tensor("ccos_sb", [128, 128], BF16)
    csinn = nc.alloc_sbuf_tensor("csinn_sb", [128, 128], BF16)
    cicos = nc.alloc_sbuf_tensor("cicos_sb", [128, 128], BF16)
    cisinn = nc.alloc_sbuf_tensor("cisinn_sb", [128, 128], BF16)
    ident = nc.alloc_sbuf_tensor("ident_sb", [128, 128], BF16)
    ones128 = nc.alloc_sbuf_tensor("ones128_sb", [128, 1], BF16)
    ones1x128 = nc.alloc_sbuf_tensor("ones1x128_sb", [1, 128], F32)

    pb = [nc.alloc_psum_tensor(f"pb{i}", [128, 512], F32) for i in range(8)]
    pb2_bf = pb[2][:].bitcast(BF16)
    pb3_bf = pb[3][:].bitcast(BF16)

    bld = Builder(nc)
    for s in (["c_ld", "wc", "xin", "yst", "sT", "sV", "sS",
               "waA", "waB", "pkg", "pk0", "pk1",
               "pst00", "pst01", "pst10", "pst11"]
              + [f"waO{i}" for i in range(4)]
              + [f"wgs{i}" for i in range(4)]
              + [f"wus{i}" for i in range(4)]
              + [f"wds{i}" for i in range(2)]):
        bld.sem(s)
    E, W, need, reads = bld.E, bld.W, bld.need, bld.reads
    cnt = bld.cnt

    for dst, src in [(ccos, ccos_in), (csinn, csinn_in), (cicos, cicos_in),
                     (cisinn, cisinn_in), (ident, ident_in),
                     (ones128, ones128_in), (ones1x128, ones1x128_in)]:
        E("sync", (lambda e, d=dst, s=src: e.dma_start(out=d[:], in_=s)), ("c_ld", 16))
    for q4 in range(4):
        if q4 > 0:
            W("gpsimd", "wc")
        E("gpsimd", (lambda e, q4=q4: e.dma_start(
            out=wq_bf[q4],
            in_=wq_in[:, 512 * q4:512 * (q4 + 1)].rearrange("(c p) n -> p c n", p=128))),
          ("wc", 16))
    for q4 in range(4):
        W("gpsimd", "wc")
        E("gpsimd", (lambda e, q4=q4: e.dma_start(
            out=wkv_bf[q4],
            in_=wkv_in[:, 512 * q4:512 * (q4 + 1)].rearrange("(c p) n -> p c n", p=128))),
          ("wc", 16))
    for c16 in range(16):
        W("gpsimd", "wc")
        E("gpsimd", (lambda e, c16=c16: e.dma_start(
            out=wo_bf[c16],
            in_=wo_in[:, 128 * c16:128 * (c16 + 1)].rearrange("(c p) n -> p c n", p=128))),
          ("wc", 16))

    W("tensor", "c_ld")

    x_flat_i = x_sb[:].rearrange("p c n -> p (c n)").bitcast(I32)
    x_flat_f = x_sb[:].rearrange("p c n -> p (c n)")
    int_flat_i = inter[:].rearrange("p c n -> p (c n)").bitcast(I32)
    int_flat_f = inter[:].rearrange("p c n -> p (c n)").bitcast(F32)
    qT_flat_f = qT[:].rearrange("p c n -> p (c n)").bitcast(F32)
    kT_flat = kT[:].rearrange("p c n -> p (c n)")

    pakb = [x_flat_i[:, 0:2048], x_flat_i[:, 2048:4096]]
    sclb = [x_flat_f[:, 4096:6144], x_flat_f[:, 6144:8192]]
    t1b = int_flat_i[:, 0:2048]
    bitsb = int_flat_i[:, 2048:4096]
    s31b = int_flat_i[:, 4096:6144]
    vmb = int_flat_f[:, 6144:8192]
    relub = qT_flat_f[:, 0:2048]
    wout = [[kT_flat[:, 0:2048], kT_flat[:, 2048:4096]],
            [kT_flat[:, 4096:6144], kT_flat[:, 6144:8192]]]

    def dequant_tile(buf, pak_src, scl_src, store_lo, store_hi):
        need("sync", ("pak", buf))
        E("sync", (lambda e, d=pakb[buf], s=pak_src: e.dma_start(out=d, in_=s)),
          (f"pk{buf}", 16))
        need("gpsimd", ("scl", buf))
        W("gpsimd", "pkg")
        E("gpsimd", (lambda e, d=sclb[buf], s=scl_src: e.dma_start(out=d, in_=s)),
          ("pkg", 16))
        W("vector", f"pk{buf}")
        W("vector", "pkg")
        for hi in (0, 1):
            dst = wout[buf][hi]
            need("vector", ("wout", buf, hi))
            p = pakb[buf]
            if hi:
                E("vector", (lambda e, p=p: e.tensor_scalar(
                    t1b, p, 4, 7, ALU.logical_shift_right, ALU.bitwise_and)), ("sV", 1))
            else:
                E("vector", (lambda e, p=p: e.tensor_scalar(
                    t1b, p, 7, None, ALU.bitwise_and)), ("sV", 1))
            W("vector", "sV")
            E("vector", lambda e: e.tensor_scalar(
                bitsb, t1b, 0x400000, 0x3F000000, ALU.mult, ALU.add), ("sV", 1))
            sv_bits = cnt["sV"]
            sh = 24 if hi else 28
            E("vector", (lambda e, sh=sh, p=p: e.tensor_scalar(
                s31b, p, sh, SIGN32, ALU.logical_shift_left, ALU.bitwise_and)), ("sV", 1))
            sv_s31 = cnt["sV"]
            W("scalar", "sV", sv_bits)
            E("scalar", lambda e: e.activation(
                relub, bitsb.bitcast(F32), AF.Relu, bias=1.0, scale=-1.0), ("sS", 1))
            W("vector", "sS")
            E("vector", lambda e: e.tensor_tensor(
                vmb, bitsb.bitcast(F32), relub, ALU.subtract), ("sV", 1))
            W("vector", "sV")
            E("vector", lambda e: e.tensor_tensor(
                vmb.bitcast(I32), vmb.bitcast(I32), s31b, ALU.bitwise_or), ("sV", 1))
            W("vector", "sV")
            E("vector", (lambda e, d=dst, sc=sclb[buf]: e.tensor_tensor(
                d, vmb, sc, ALU.mult)), ("sV", 1))
            reads(("pak", buf), "sV")
            reads(("scl", buf), "sV")
            sv_w = cnt["sV"]
            W("sync", "sV", sv_w)
            E("sync", (lambda e, d=(store_hi if hi else store_lo), s=dst:
                       e.dma_start(out=d, in_=s)), (f"pst{buf}{hi}", 16))
            reads(("wout", buf, hi), f"pst{buf}{hi}")

    tilecnt = 0
    for pk_t, sc_t, w_t in ((gpak_in, gscl_in, wg_bf),
                            (upak_in, uscl_in, wu_bf)):
        for kc in range(8):
            for ns in range(8):
                pak_src = pk_t[128 * kc:128 * (kc + 1), 2048 * ns:2048 * (ns + 1)]
                scl_src = bass.AP(sc_t.tensor, (8 * kc) * NMLP + 2048 * ns,
                                  [[NMLP, 8], [0, 16], [1, 2048]])
                lo = w_t[16 * ns:16 * (ns + 1), :, kc, :].rearrange("a p n -> p a n")
                hi = w_t[16 * ns:16 * (ns + 1), :, kc + 8, :].rearrange("a p n -> p a n")
                dequant_tile(tilecnt % 2, pak_src, scl_src, lo, hi)
                tilecnt += 1
    for pc in range(64):
        pak_src = dpak_in[128 * pc:128 * (pc + 1), :]
        scl_src = bass.AP(dscl_in.tensor, (8 * pc) * D, [[D, 8], [0, 16], [1, 2048]])
        lo = wd_bf[:, pc // 32, :, pc % 32, :].rearrange("a p h -> p a h")
        hi = wd_bf[:, (pc + 64) // 32, :, (pc + 64) % 32, :].rearrange("a p h -> p a h")
        dequant_tile(tilecnt % 2, pak_src, scl_src, lo, hi)
        tilecnt += 1

    bld.guard.setdefault("x_sb", []).append(("sV", cnt["sV"]))
    pst_totals = [(f"pst{b}{h}", cnt[f"pst{b}{h}"]) for b in (0, 1) for h in (0, 1)]

    wsA_v = wsA[:].rearrange("p (c n) -> p c n", c=16)
    wsB_v = wsB[:].rearrange("p (c n) -> p c n", c=16)
    W("sync", "wc", 8 * 16)
    for j, src2 in enumerate([wq_bf, wkv_bf]):
        for q in range(4):
            need("sync", "wsA")
            E("sync", (lambda e, s=src2, q=q: e.dma_start(
                out=wsA_v, in_=s[q])), ("waA", 16))
            W("tensor", "waA")
            need("tensor", "pb6")
            for kc in range(NKC):
                E("tensor", (lambda e, kc=kc: e.matmul(
                    pb[6][0:1, :], ones128[:], wsA_v[:, kc, :],
                    start=(kc == 0), stop=(kc == NKC - 1))),
                  *((("sT", 1),) if kc == NKC - 1 else ()))
            reads("wsA", "sT")
            W("vector", "sT")
            E("vector", (lambda e, j=j, q=q: e.tensor_copy(
                wbar[0:1, j * D + 512 * q:j * D + 512 * (q + 1)],
                pb[6][0:1, :])), ("sV", 1))
            reads("pb6", "sV")

    wg_ring = [wsA[:, 2048 * i:2048 * (i + 1)].rearrange("p (c n) -> p c n", c=16)
               for i in range(4)]
    wu_ring = [wsB[:, 2048 * i:2048 * (i + 1)].rearrange("p (c n) -> p c n", c=16)
               for i in range(4)]
    wd_ring = [wsC[:, 4096 * i:4096 * (i + 1)].rearrange("p (a n) -> p a n", a=32)
               for i in range(2)]
    wo_ring = [wsC[:, 2048 * i:2048 * (i + 1)].rearrange("p (c n) -> p c n", c=16)
               for i in range(4)]

    first_ffn = [True]

    for tile in range(NTILE):
        b = tile // 4
        s0 = (tile % 4) * TT

        W("sync", "yst", 16 * tile)
        need("sync", "x_sb")
        E("sync", (lambda e, b=b, s0=s0: e.dma_start(
            out=x_sb[:],
            in_=x_in[b, :, s0:s0 + TT].rearrange("(c p) n -> p c n", p=128))),
          ("xin", 16))

        W("scalar", "xin")
        need("scalar", "b1")
        for c in range(NKC):
            E("scalar", (lambda e, c=c: e.copy(b1[:, c, :], x_sb[:, c, :])),
              ("sS", 1))
        sS_cast = cnt["sS"]
        W("tensor", "sS", sS_cast)
        need("tensor", "pb6")
        for c in range(NKC):
            E("tensor", (lambda e, c=c: e.matmul(
                pb[6][0:1, :], ones128[:], b1[:, c, :],
                start=(c == 0), stop=(c == NKC - 1))),
              *((("sT", 1),) if c == NKC - 1 else ()))
        W("vector", "sT")
        need("vector", "m_neg")
        if tile == 0:
            for _s, _v in pst_totals:
                W("vector", _s, _v)
        E("vector", lambda e: e.tensor_scalar(
            m_neg[:], pb[6][0:1, :], -1.0 / D, None, ALU.mult), ("sV", 1))
        reads("pb6", "sV")
        sV_mneg = cnt["sV"]

        for q in range(4):
            need("sync", "wsA")
            E("sync", (lambda e, q=q: e.dma_start(out=wsA_v, in_=wq_bf[q])),
              ("waA", 16))
            need("sync", "wsB")
            E("sync", (lambda e, q=q: e.dma_start(out=wsB_v, in_=wkv_bf[q])),
              ("waB", 16))
            wa_qa, wa_qb = cnt["waA"], cnt["waB"]
            for tc in range(TT // 128):
                tsl = slice(128 * tc, 128 * (tc + 1))
                W("tensor", "waA", wa_qa)
                W("tensor", "waB", wa_qb)
                W("tensor", "sV", sV_mneg)
                need("tensor", "pb0")
                need("tensor", "pb1")
                for kc in range(NKC):
                    E("tensor", (lambda e, kc=kc, tsl=tsl: e.matmul(
                        pb[0][:, :], b1[:, kc, tsl], wsA_v[:, kc, :],
                        start=(kc == 0), stop=False)))
                E("tensor", (lambda e, tsl=tsl, q=q: e.matmul(
                    pb[0][:, :], m_neg[0:1, tsl], wbar[0:1, 512 * q:512 * (q + 1)],
                    start=False, stop=True)), ("sT", 1))
                sT_q = cnt["sT"]
                for kc in range(NKC):
                    E("tensor", (lambda e, kc=kc, tsl=tsl: e.matmul(
                        pb[1][:, :], b1[:, kc, tsl], wsB_v[:, kc, :],
                        start=(kc == 0), stop=False)))
                E("tensor", (lambda e, tsl=tsl, q=q: e.matmul(
                    pb[1][:, :], m_neg[0:1, tsl],
                    wbar[0:1, D + 512 * q:D + 512 * (q + 1)],
                    start=False, stop=True)), ("sT", 1))
                sT_kv = cnt["sT"]
                W("scalar", "sT", sT_q)
                need("scalar", "sq_q")
                E("scalar", lambda e: e.activation(sq_q[:], pb[0][:, :], AF.Square),
                  ("sS", 1))
                sS_sqq = cnt["sS"]
                W("scalar", "sT", sT_kv)
                need("scalar", "sq_k")
                E("scalar", lambda e: e.activation(sq_k[:], pb[1][:, :], AF.Square),
                  ("sS", 1))
                sS_sqk = cnt["sS"]
                W("vector", "sS", sS_sqq)
                W("vector", "sV")
                E("vector", lambda e: e.tensor_reduce(
                    ssq[:, 0:4], sq_q[:].rearrange("p (g n) -> p g n", g=4),
                    AX.X, ALU.add), ("sV", 1))
                reads("sq_q", "sV")
                W("vector", "sS", sS_sqk)
                W("vector", "sV")
                E("vector", lambda e: e.tensor_reduce(
                    ssq[:, 4:8], sq_k[:].rearrange("p (g n) -> p g n", g=4),
                    AX.X, ALU.add), ("sV", 1))
                reads("sq_k", "sV")

                def nstep(fn):
                    W("vector", "sV")
                    E("vector", fn, ("sV", 1))
                nstep(lambda e: e.tensor_scalar(
                    ny[:].bitcast(I32), ssq[:].bitcast(I32), 1, None,
                    ALU.logical_shift_right))
                nstep(lambda e: e.tensor_scalar(
                    ny[:].bitcast(I32), ny[:].bitcast(I32), -1, MAGIC,
                    ALU.mult, ALU.add))
                for _ in range(2):
                    nstep(lambda e: e.tensor_tensor(na[:], ny[:], ny[:], ALU.mult))
                    nstep(lambda e: e.tensor_tensor(na[:], na[:], ssq[:], ALU.mult))
                    nstep(lambda e: e.tensor_scalar(nh[:], na[:], -0.5, 1.5,
                                                    ALU.mult, ALU.add))
                    nstep(lambda e: e.tensor_tensor(ny[:], ny[:], nh[:], ALU.mult))
                nstep(lambda e: e.tensor_scalar(rqk[:], ny[:], INV_SQRT_HD, None,
                                                ALU.mult))
                need("vector", "qtok")
                need("vector", "ktok")
                for h4 in range(4):
                    hs = slice(128 * h4, 128 * (h4 + 1))
                    W("vector", "sV")
                    E("vector", (lambda e, h4=h4, hs=hs: e.tensor_scalar(
                        qtok[:, hs], pb[0][:, hs], rqk[:, h4:h4 + 1], None, ALU.mult)),
                      ("sV", 1))
                for h4 in range(4):
                    hs = slice(128 * h4, 128 * (h4 + 1))
                    W("vector", "sV")
                    E("vector", (lambda e, h4=h4, hs=hs: e.tensor_scalar(
                        ktok[:, hs], pb[1][:, hs], rqk[:, 4 + h4:5 + h4], None,
                        ALU.mult)), ("sV", 1))
                reads("pb0", "sV")
                reads("pb1", "sV")
                sV_tok = cnt["sV"]
                W("tensor", "sV", sV_tok)
                need("tensor", "pb2")
                need("tensor", "pb3")
                for h4 in range(4):
                    hs = slice(128 * h4, 128 * (h4 + 1))
                    E("tensor", (lambda e, hs=hs: e.transpose(
                        pb2_bf[:, hs], qtok[:, hs], ident[:])))
                for h4 in range(4):
                    hs = slice(128 * h4, 128 * (h4 + 1))
                    E("tensor", (lambda e, hs=hs: e.transpose(
                        pb3_bf[:, hs], ktok[:, hs], ident[:])),
                      *((("sT", 1),) if h4 == 3 else ()))
                reads("qtok", "sT")
                reads("ktok", "sT")
                sT_tr = cnt["sT"]
                W("vector", "sT", sT_tr)
                E("vector", (lambda e, q=q, tsl=tsl: e.tensor_copy(
                    qT[:, 4 * q:4 * (q + 1), tsl],
                    pb2_bf[:, 0:512].rearrange("p (a n) -> p a n", a=4))), ("sV", 1))
                reads("pb2", "sV")
                W("vector", "sV")
                E("vector", (lambda e, q=q, tsl=tsl: e.tensor_copy(
                    kT[:, 4 * q + 1:4 * q + 5, tsl],
                    pb3_bf[:, 0:512].rearrange("p (a n) -> p a n", a=4))), ("sV", 1))
                reads("pb3", "sV")
            reads("wsA", "sT")
            reads("wsB", "sT")
        for s4 in range(4):
            reads(("wg", s4), "sT")
            reads(("wu", s4), "sT")

        if tile % 4 == 0:
            W("vector", "sV")
            E("vector", lambda e: e.memset(carry[:], 0.0), ("sV", 1))
        for h in range(H):
            W("vector", "sV")
            E("vector", (lambda e, h=h: e.tensor_tensor_scan(
                kT[:, h, :], kT[:, h + 1, :], kT[:, h + 1, :], carry[:, h:h + 1],
                ALU.add, ALU.bypass)), ("sV", 1))
            W("vector", "sV")
            E("vector", (lambda e, h=h: e.tensor_reduce(
                cred[:], kT[:, h + 1, :], AX.X, ALU.add)), ("sV", 1))
            W("vector", "sV")
            E("vector", (lambda e, h=h: e.tensor_tensor(
                carry[:, h:h + 1], carry[:, h:h + 1], cred[:], ALU.add)), ("sV", 1))
        sV_scan = cnt["sV"]

        for q in range(4):
            hsl = slice(4 * q, 4 * (q + 1))
            for tc in range(TT // 128):
                tsl = slice(128 * tc, 128 * (tc + 1))
                W("tensor", "sV", sV_scan)
                need("tensor", "pb0")
                need("tensor", "pb1")
                need("tensor", "pb4")
                need("tensor", "pb5")
                E("tensor", (lambda e, hsl=hsl, tsl=tsl: e.matmul(
                    pb[0][:, :], ccos[:], qT[:, hsl, tsl], start=True, stop=True)),
                  ("sT", 1))
                E("tensor", (lambda e, hsl=hsl, tsl=tsl: e.matmul(
                    pb[1][:, :], csinn[:], qT[:, hsl, tsl], start=True, stop=True)),
                  ("sT", 1))
                E("tensor", (lambda e, hsl=hsl, tsl=tsl: e.matmul(
                    pb[4][:, :], ccos[:], kT[:, hsl, tsl], start=True, stop=True)),
                  ("sT", 1))
                E("tensor", (lambda e, hsl=hsl, tsl=tsl: e.matmul(
                    pb[5][:, :], csinn[:], kT[:, hsl, tsl], start=True, stop=True)),
                  ("sT", 1))
                sT_fft = cnt["sT"]
                W("scalar", "sT", sT_fft)
                need("scalar", "qfr")
                E("scalar", lambda e: e.copy(qfr[:], pb[0][:, :]), ("sS", 1))
                reads("pb0", "sS")
                need("scalar", "qfi")
                E("scalar", lambda e: e.copy(qfi[:], pb[1][:, :]), ("sS", 1))
                reads("pb1", "sS")
                sS_qf = cnt["sS"]
                W("vector", "sT", sT_fft)
                need("vector", "kfr")
                E("vector", lambda e: e.tensor_copy(kfr[:], pb[4][:, :]), ("sV", 1))
                reads("pb4", "sV")
                need("vector", "kfi")
                W("vector", "sV")
                E("vector", lambda e: e.tensor_copy(kfi[:], pb[5][:, :]), ("sV", 1))
                reads("pb5", "sV")
                W("vector", "sS", sS_qf)
                W("vector", "sV")
                E("vector", lambda e: e.tensor_tensor(qvr[:], qfr[:], kfr[:],
                                                      ALU.mult), ("sV", 1))
                W("vector", "sV")
                E("vector", lambda e: e.tensor_tensor(qvi[:], qfi[:], kfi[:],
                                                      ALU.mult), ("sV", 1))
                W("vector", "sV")
                E("vector", lambda e: e.tensor_tensor(qvr[:], qvr[:], qvi[:],
                                                      ALU.subtract), ("sV", 1))
                W("vector", "sV")
                E("vector", lambda e: e.tensor_tensor(qvt[:], qfr[:], kfi[:],
                                                      ALU.mult), ("sV", 1))
                W("vector", "sV")
                E("vector", lambda e: e.tensor_tensor(qvi[:], qfi[:], kfr[:],
                                                      ALU.mult), ("sV", 1))
                W("vector", "sV")
                E("vector", lambda e: e.tensor_tensor(qvi[:], qvi[:], qvt[:],
                                                      ALU.add), ("sV", 1))
                reads("qfr", "sV")
                reads("qfi", "sV")
                reads("kfr", "sV")
                reads("kfi", "sV")
                sV_qv = cnt["sV"]
                W("tensor", "sV", sV_qv)
                need("tensor", "pb2")
                E("tensor", lambda e: e.matmul(pb[2][:, :], cicos[:], qvr[:],
                                               start=True, stop=False))
                E("tensor", lambda e: e.matmul(pb[2][:, :], cisinn[:], qvi[:],
                                               start=False, stop=True), ("sT", 1))
                reads("qvr", "sT")
                reads("qvi", "sT")
                sT_ifft = cnt["sT"]
                W("scalar", "sT", sT_ifft)
                E("scalar", (lambda e, hsl=hsl, tsl=tsl: e.copy(
                    b1[:, hsl, tsl],
                    pb[2][:, :].rearrange("p (a n) -> p a n", a=4))), ("sS", 1))
                reads("pb2", "sS")
        sS_vh = cnt["sS"]

        if tile == 0:
            W("sync", "wc", 24 * 16)
        for do in range(16):
            slot = do % 4
            need("sync", ("wo", slot))
            E("sync", (lambda e, do=do, slot=slot: e.dma_start(
                out=wo_ring[slot], in_=wo_bf[do])), (f"waO{slot}", 16))
            wa_wo = cnt[f"waO{slot}"]
            W("tensor", f"waO{slot}", wa_wo)
            W("tensor", "sS", sS_vh)
            pbk = 0 if do % 2 == 0 else 1
            need("tensor", f"pb{pbk}")
            for dh in range(16):
                E("tensor", (lambda e, dh=dh, slot=slot, pbk=pbk: e.matmul(
                    pb[pbk][:, :], wo_ring[slot][:, dh, :], b1[:, dh, :],
                    start=(dh == 0), stop=(dh == 15))),
                  *((("sT", 1),) if dh == 15 else ()))
            reads(("wo", slot), "sT")
            sT_o = cnt["sT"]
            W("vector", "sT", sT_o)
            E("vector", (lambda e, do=do, pbk=pbk: e.tensor_tensor(
                x_sb[:, do, :], x_sb[:, do, :], pb[pbk][:, :], ALU.add)), ("sV", 1))
            reads(f"pb{pbk}", "sV")
        sV_oproj = cnt["sV"]
        reads(("wd", 0), "sT")
        reads(("wd", 1), "sT")

        for c in range(NKC):
            par = c % 2
            W("scalar", "sV", sV_oproj)
            need("scalar", ("xb2", par))
            E("scalar", (lambda e, c=c, par=par: e.copy(xb2[:, par, :],
                                                        x_sb[:, c, :])), ("sS", 1))
            need("scalar", ("xsq", par))
            E("scalar", (lambda e, c=c, par=par: e.activation(
                xsq[:, par, :], x_sb[:, c, :], AF.Square)), ("sS", 1))
            sS_c = cnt["sS"]
            W("tensor", "sS", sS_c)
            if c == 0:
                need("tensor", "pb6")
                need("tensor", "pb7")
            E("tensor", (lambda e, par=par, c=c: e.matmul(
                pb[6][0:1, :], ones128[:], xb2[:, par, :],
                start=(c == 0), stop=(c == NKC - 1))), ("sT", 1))
            reads(("xb2", par), "sT")
            E("tensor", (lambda e, par=par, c=c: e.matmul(
                pb[7][0:1, :], ones128[:], xsq[:, par, :],
                start=(c == 0), stop=(c == NKC - 1))), ("sT", 1))
            reads(("xsq", par), "sT")
        sT_stats = cnt["sT"]
        W("vector", "sT", sT_stats)
        need("vector", "meanf")
        E("vector", lambda e: e.tensor_scalar(meanf, pb[6][0:1, :], 1.0 / D,
                                              None, ALU.mult), ("sV", 1))
        reads("pb6", "sV")
        W("vector", "sV")
        E("vector", lambda e: e.tensor_scalar(msq, pb[7][0:1, :], 1.0 / D,
                                              None, ALU.mult), ("sV", 1))
        reads("pb7", "sV")
        W("vector", "sV")
        E("vector", lambda e: e.tensor_tensor(varb, meanf, meanf,
                                              ALU.mult), ("sV", 1))
        W("vector", "sV")
        E("vector", lambda e: e.tensor_tensor(varb, msq, varb,
                                              ALU.subtract), ("sV", 1))
        W("vector", "sV")
        E("vector", lambda e: e.tensor_scalar(varb, varb, 1.0, 1e-5,
                                              ALU.mult, ALU.add), ("sV", 1))

        def lnstep(fn):
            W("vector", "sV")
            E("vector", fn, ("sV", 1))
        lnstep(lambda e: e.tensor_scalar(
            l_ny.bitcast(I32), varb.bitcast(I32), 1, None,
            ALU.logical_shift_right))
        lnstep(lambda e: e.tensor_scalar(
            l_ny.bitcast(I32), l_ny.bitcast(I32), -1, MAGIC,
            ALU.mult, ALU.add))
        for _ in range(2):
            lnstep(lambda e: e.tensor_tensor(l_na, l_ny, l_ny, ALU.mult))
            lnstep(lambda e: e.tensor_tensor(l_na, l_na, varb, ALU.mult))
            lnstep(lambda e: e.tensor_scalar(l_nh, l_na, -0.5, 1.5,
                                             ALU.mult, ALU.add))
            lnstep(lambda e: e.tensor_tensor(l_ny, l_ny, l_nh, ALU.mult))
        need("vector", "rstd")
        lnstep(lambda e: e.tensor_copy(rstd[:], l_ny))
        sV_rstd = cnt["sV"]
        W("tensor", "sV", sV_rstd)
        need("tensor", "pb2")
        need("tensor", "pb3")
        E("tensor", lambda e: e.matmul(pb[2][:, :], ones1x128[:], meanf,
                                       start=True, stop=True))
        E("tensor", lambda e: e.matmul(pb[3][:, :], ones1x128[:], rstd[:],
                                       start=True, stop=True), ("sT", 1))
        reads("meanf", "sT")
        reads("rstd", "sT")
        sT_bc = cnt["sT"]
        W("vector", "sT", sT_bc)
        for c in range(NKC):
            W("vector", "sV")
            E("vector", (lambda e, c=c: e.tensor_tensor(
                tmp1[:], x_sb[:, c, :], pb[2][:, :], ALU.subtract)), ("sV", 1))
            W("vector", "sV")
            E("vector", (lambda e, c=c: e.tensor_tensor(
                b1[:, c, :], tmp1[:], pb[3][:, :], ALU.mult)), ("sV", 1))
        reads("pb2", "sV")
        reads("pb3", "sV")
        sV_h2 = cnt["sV"]

        if first_ffn[0]:
            for _s, _v in pst_totals:
                W("sync", _s, _v)
            first_ffn[0] = False
        for sup in range(4):
            for i in range(32):
                ncx = 32 * sup + i
                slot = ncx % 4
                need("sync", ("wg", slot))
                E("sync", (lambda e, ncx=ncx, slot=slot: e.dma_start(
                    out=wg_ring[slot], in_=wg_bf[ncx])), (f"wgs{slot}", 16))
                need("sync", ("wu", slot))
                E("sync", (lambda e, ncx=ncx, slot=slot: e.dma_start(
                    out=wu_ring[slot], in_=wu_bf[ncx])), (f"wus{slot}", 16))
                wf_g, wf_u = cnt[f"wgs{slot}"], cnt[f"wus{slot}"]
                gp = 0 if i % 2 == 0 else 4
                up = 1 if i % 2 == 0 else 5
                W("tensor", f"wgs{slot}", wf_g)
                W("tensor", f"wus{slot}", wf_u)
                W("tensor", "sV", sV_h2)
                need("tensor", f"pb{gp}")
                for kc in range(NKC):
                    E("tensor", (lambda e, kc=kc, slot=slot, gp=gp: e.matmul(
                        pb[gp][:, :], wg_ring[slot][:, kc, :], b1[:, kc, :],
                        start=(kc == 0), stop=(kc == NKC - 1))),
                      *((("sT", 1),) if kc == NKC - 1 else ()))
                sT_g = cnt["sT"]
                need("tensor", f"pb{up}")
                for kc in range(NKC):
                    E("tensor", (lambda e, kc=kc, slot=slot, up=up: e.matmul(
                        pb[up][:, :], wu_ring[slot][:, kc, :], b1[:, kc, :],
                        start=(kc == 0), stop=(kc == NKC - 1))),
                      *((("sT", 1),) if kc == NKC - 1 else ()))
                reads(("wg", slot), "sT")
                reads(("wu", slot), "sT")
                sT_u = cnt["sT"]
                W("scalar", "sT", sT_g)
                par = i % 2
                need("scalar", ("gel", par))
                E("scalar", (lambda e, gp=gp, par=par: e.activation(
                    gel[:, par, :], pb[gp][:, :], AF.Gelu_apprx_tanh)), ("sS", 1))
                reads(f"pb{gp}", "sS")
                sS_gel = cnt["sS"]
                W("vector", "sT", sT_u)
                W("vector", "sS", sS_gel)
                E("vector", (lambda e, i=i, up=up, par=par: e.tensor_tensor(
                    inter[:, i, :], gel[:, par, :], pb[up][:, :], ALU.mult)),
                  ("sV", 1))
                reads(f"pb{up}", "sV")
                reads(("gel", par), "sV")
            sV_inter = cnt["sV"]
            for do in range(16):
                slot = do % 2
                need("sync", ("wd", slot))
                E("sync", (lambda e, do=do, sup=sup, slot=slot: e.dma_start(
                    out=wd_ring[slot], in_=wd_bf[do, sup])), (f"wds{slot}", 16))
                wf_do = cnt[f"wds{slot}"]
                dp = 2 if do % 2 == 0 else 3
                W("tensor", f"wds{slot}", wf_do)
                W("tensor", "sV", sV_inter)
                need("tensor", f"pb{dp}")
                for j in range(32):
                    E("tensor", (lambda e, j=j, slot=slot, dp=dp: e.matmul(
                        pb[dp][:, :], wd_ring[slot][:, j, :], inter[:, j, :],
                        start=(j == 0), stop=(j == 31))),
                      *((("sT", 1),) if j == 31 else ()))
                reads(("wd", slot), "sT")
                sT_d = cnt["sT"]
                W("vector", "sT", sT_d)
                E("vector", (lambda e, do=do, dp=dp: e.tensor_tensor(
                    x_sb[:, do, :], x_sb[:, do, :], pb[dp][:, :], ALU.add)),
                  ("sV", 1))
                reads(f"pb{dp}", "sV")
        sV_ffn = cnt["sV"]
        reads("wsA", "sT")
        reads("wsB", "sT")
        for s4 in range(4):
            reads(("wo", s4), "sT")
            reads(("wg", s4), "sT")
            reads(("wu", s4), "sT")

        W("sync", "sV", sV_ffn)
        E("sync", (lambda e, b=b, s0=s0: e.dma_start(
            out=y_out[b, :, s0:s0 + TT].rearrange("(c p) n -> p c n", p=128),
            in_=x_sb[:])), ("yst", 16))
        reads("x_sb", "yst")
        bld.guard.setdefault("b1", []).append(("sT", cnt["sT"]))
        bld.guard.setdefault("m_neg", []).append(("sT", cnt["sT"]))

    W("sync", "yst", 16 * NTILE)

    with nc.Block() as block:
        bld.replay(block)
    return nc


_NC_CACHE = [None]


def kernel(**inputs):
    x = np.asarray(inputs["L0_attn_x"])
    wq = np.asarray(inputs["L0_attn_W_q"])
    wkv = np.asarray(inputs["L0_attn_W_kv"])
    wo = np.asarray(inputs["L0_attn_W_o"])
    gpakf = np.asarray(inputs["L0_ffn_gate_W_packed"])
    gsclf = np.asarray(inputs["L0_ffn_gate_scales"])
    upakf = np.asarray(inputs["L0_ffn_up_W_packed"])
    usclf = np.asarray(inputs["L0_ffn_up_scales"])
    dpakf = np.asarray(inputs["L0_ffn_down_W_packed"])
    dsclf = np.asarray(inputs["L0_ffn_down_scales"])

    x_dev = np.ascontiguousarray(x.transpose(0, 2, 1)[:, PERM_D, :])
    wq_dev = np.ascontiguousarray(wq.T[PERM_D])
    wkv_dev = np.ascontiguousarray(wkv.T[PERM_D])
    wo_dev = np.ascontiguousarray(wo[PERM_D].T)
    gpak_dev = np.ascontiguousarray(gpakf[PERM_N].T)
    gscl_dev = np.ascontiguousarray(gsclf[PERM_N].T)
    upak_dev = np.ascontiguousarray(upakf[PERM_N].T)
    uscl_dev = np.ascontiguousarray(usclf[PERM_N].T)
    dpak_dev = np.ascontiguousarray(dpakf[PERM_D].T)
    dscl_dev = np.ascontiguousarray(dsclf[PERM_D].T)

    ccos, csinn, cicos, cisinn, ident, ones128, ones1x128 = _consts()

    if _NC_CACHE[0] is None:
        _NC_CACHE[0] = build_program()
    nc = _NC_CACHE[0]

    in_maps = []
    for c in range(8):
        in_maps.append({
            "x": x_dev[2 * c:2 * c + 2],
            "wqT": wq_dev, "wkvT": wkv_dev, "woT": wo_dev,
            "gpak": gpak_dev, "gscl": gscl_dev,
            "upak": upak_dev, "uscl": uscl_dev,
            "dpak": dpak_dev, "dscl": dscl_dev,
            "ccos": ccos, "csinn": csinn, "cicos": cicos, "cisinn": cisinn,
            "ident": ident, "ones128": ones128, "ones1x128": ones1x128,
        })
    res = run_bass_kernel_spmd(nc, in_maps, core_ids=list(range(8)))
    y_dev = np.concatenate([res.results[c]["y"] for c in range(8)], axis=0)
    out = np.empty((16, S, D), dtype=np.float32)
    out[:, :, PERM_D] = y_dev.transpose(0, 2, 1)
    return out


if __name__ == "__main__":
    import reference as R
    inp = R.setup_inputs()
    inp = {k: np.asarray(v) for k, v in inp.items()}
    got = kernel(**inp)
    exp = np.asarray(R.reference(**inp))
    err = np.linalg.norm(got - exp) / np.linalg.norm(exp)
    print("rel l2:", err)
